# version 03
# baseline (speedup 1.0000x reference)
import numpy as np
import ml_dtypes
from contextlib import ExitStack

import concourse.bass as bass
import concourse.mybir as mybir
from concourse.bass_utils import run_bass_kernel_spmd
from concourse.alu_op_type import AluOpType as OP

F32 = mybir.dt.float32
BF16 = mybir.dt.bfloat16
AF = mybir.ActivationFunctionType
AX = mybir.AxisListType

D = 2048
I = 4096
HID = 5461
NH = 64
HP = 64
NS = 128
T = 1024
CK = 128
NCK = T // CK
KCV = 4
CONVD = I + 2 * NS
PROJ = I + CONVD + NH
NFC = CONVD // 128
EPS = 1e-5
W1CNT = float(HID * D)
W2CNT = float(D * HID)

BF = ml_dtypes.bfloat16


class Pool:

    def __init__(self, nc):
        self.nc = nc
        self.items = []
        self.size = 0

    def at(self, off, name, shape, dtype):
        per_part = int(np.prod(shape[1:])) * mybir.dt.size(dtype)
        assert off % 32 == 0, (name, off)
        h = self.nc.virtual_sbuf_tensor(name, shape, dtype)
        self.items.append((h, off))
        self.size = max(self.size, off + per_part)
        return h

    def finalize(self, ctx):
        big = ctx.enter_context(
            self.nc.sbuf_tensor("pool", [128, self.size], mybir.dt.uint8)
        )
        base = self.nc.lookup_mloc(big).addr
        for h, off in self.items:
            real = self.nc.alloc_sbuf_tensor_at(
                h.name + "_bk", h.shape, h.dtype, offset=base + off)
            self.nc.back_virtual_tensor_with(h, real)
        return big


class Sem:
    def __init__(self, nc, ctx, name):
        self.h = ctx.enter_context(nc.semaphore(name))
        self.n = 0

    def bump(self, amt):
        self.n += amt
        return self.n


class Prog:

    ENGS = ("sync", "scalar", "vector", "tensor", "gpsimd")

    def __init__(self, nc):
        self.nc = nc
        self.ops = {e: [] for e in self.ENGS}
        self.ninstr = 0

    def do(self, eng, fn, waits=(), inc=None):
        wl = [(s.h, int(v)) for (s, v) in waits if v > 0]
        il = None
        post = None
        if inc is not None:
            s, amt = inc
            post = s.bump(amt)
            il = (s.h, amt)
        self.ninstr += 1

        def closure(engine, fn=fn, wl=wl, il=il):
            for h, v in wl:
                engine.wait_ge(h, v)
            instr = fn(engine)
            if il is not None:
                instr.then_inc(il[0], il[1])

        self.ops[eng].append(closure)
        return post

    def wait(self, eng, sem, value):
        if value <= 0:
            return
        self.ops[eng].append(lambda engine, h=sem.h, v=int(value): engine.wait_ge(h, v))

    def emit(self, block):
        for eng in self.ENGS:
            if not self.ops[eng]:
                continue
            seq = self.ops[eng]

            def body(engine, seq=seq):
                for fn in seq:
                    fn(engine)

            getattr(block, eng)(body)


def ceil_div(a, b):
    return (a + b - 1) // b



def build(r_dbg=(), nohalo=False, stopat=99):
    nc = bass.Bass("TRN2", target_bir_lowering=False)
    ctx = ExitStack()

    xh = nc.dram_tensor("xh", [T, D], F32, kind="ExternalInput").ap()
    xo = nc.dram_tensor("xo", [T, D], F32, kind="ExternalInput").ap()
    w_in = nc.dram_tensor("w_in", [PROJ, D], F32, kind="ExternalInput").ap()
    w_out = nc.dram_tensor("w_out", [D, I], F32, kind="ExternalInput").ap()
    w1 = nc.dram_tensor("w1", [HID, D], F32, kind="ExternalInput").ap()
    w3 = nc.dram_tensor("w3", [HID, D], F32, kind="ExternalInput").ap()
    w2 = nc.dram_tensor("w2", [D, HID], F32, kind="ExternalInput").ap()
    conv_wt = nc.dram_tensor("conv_wt", [128, NFC, KCV], F32, kind="ExternalInput").ap()
    conv_bt = nc.dram_tensor("conv_bt", [128, NFC], F32, kind="ExternalInput").ap()
    dt_bias = nc.dram_tensor("dt_bias", [NH, 1], F32, kind="ExternalInput").ap()
    a_log = nc.dram_tensor("a_log", [NH, 1], F32, kind="ExternalInput").ap()
    d_rep = nc.dram_tensor("d_rep", [1, I], F32, kind="ExternalInput").ap()
    norm_wt_d = nc.dram_tensor("norm_wt", [128, 32], F32, kind="ExternalInput").ap()
    maskd = nc.dram_tensor("maskd", [128, 1], F32, kind="ExternalInput").ap()
    idf_d = nc.dram_tensor("idf", [128, 128], F32, kind="ExternalInput").ap()
    idb_d = nc.dram_tensor("idb", [128, 128], BF16, kind="ExternalInput").ap()
    m1_d = nc.dram_tensor("m1", [128, 128], F32, kind="ExternalInput").ap()
    sel_d = nc.dram_tensor("sel32", [64, 2048], BF16, kind="ExternalInput").ap()
    onesd = nc.dram_tensor("onesd", [2, 1024], F32, kind="ExternalInput").ap()
    zerosd = nc.dram_tensor("zerosd", [128, 2048], BF16, kind="ExternalInput").ap()
    out_d = nc.dram_tensor("out", [T, D], F32, kind="ExternalOutput").ap()
    dbg_outs = {}
    def dbg_tensor(name, shape, dt):
        if name in r_dbg:
            dbg_outs[name] = nc.dram_tensor("dbg_" + name, shape, dt, kind="ExternalOutput").ap()

    dbg_tensor("stats", [128, 16], F32)
    dbg_tensor("xnfm", [128, 16, T], BF16)
    dbg_tensor("dt_raw", [NH, T], F32)
    dbg_tensor("B_fm", [128, T], BF16)
    dbg_tensor("C_fm", [128, T], BF16)
    dbg_tensor("xs0", [128, NCK, 2048], BF16)
    dbg_tensor("st0", [128, 32, HP], BF16)
    dbg_tensor("st1", [128, 32, HP], BF16)
    dbg_tensor("cbtm", [128, 128], BF16)
    dbg_tensor("s16", [128, 1024], BF16)
    dbg_tensor("xd32", [128, 2048], BF16)
    dbg_tensor("eax32", [128, 2048], BF16)
    dbg_tensor("dteb", [NH, 128], BF16)
    dbg_tensor("acum", [NH, 128], F32)
    dbg_tensor("nal", [NH, 128], F32)
    dbg_tensor("emin", [128, 1024], F32)
    dbg_tensor("Y", [128, NCK, I], BF16)
    dbg_tensor("Yg", [128, NCK, I], BF16)
    dbg_tensor("sc", [128, 12], F32)
    dbg_tensor("g", [128, 22, T], BF16)
    dbg_tensor("SUM", [128, 16, T], F32)
    dbg_tensor("SUMF", [128, 16, T], F32)

    pool = Pool(nc)
    o = 0
    xn_fm = pool.at(o, "xn_fm", [128, 16, T], BF16); o += 32768
    dt_raw = pool.at(o, "dt_raw", [NH, T], F32); O_DTRAW = o; o += 4096
    B_fm = pool.at(o, "B_fm", [128, T], BF16); o += 2048
    C_fm = pool.at(o, "C_fm", [128, T], BF16); o += 2048
    STATES = [None, None]
    STATES[0] = pool.at(o, "states0", [128, 32, HP], BF16); O_ST = o; o += 4096
    STATES[1] = pool.at(o, "states1", [128, 32, HP], BF16); o += 4096
    SEL32 = pool.at(o, "sel32s", [64, 2048], BF16); O_SEL = o; o += 4096
    DexpT = pool.at(o, "dexpt", [128, I], BF16); O_DEXPT = o; o += 8192
    M1 = pool.at(o, "m1s", [128, 128], F32); o += 512
    IDF = pool.at(o, "idf_s", [128, 128], F32); o += 512
    IDB = pool.at(o, "idb_s", [128, 128], BF16); o += 256
    CWT = pool.at(o, "cwt", [128, NFC, KCV], F32); o += 544
    CBT = pool.at(o, "cbt", [128, NFC], F32); o += 160
    NWT = pool.at(o, "nwt", [128, 32], F32); o += 128
    DTB = pool.at(o, "dtb", [NH, 1], F32); o += 32
    ALOG = pool.at(o, "alog", [NH, 1], F32); o += 32
    AEXP = pool.at(o, "aexp", [NH, 1], F32); o += 32
    MSK = pool.at(o, "msk", [128, 1], F32); o += 32
    ONES64 = pool.at(o, "ones64", [NH, 128], F32); o += 512
    ZERO64 = pool.at(o, "zero64", [NH, 128], F32); o += 512
    ONESROW = pool.at(o, "onesrow", [1, 128], F32); o += 512
    ONESROWB = pool.at(o, "onesrowb", [1, 128], BF16); o += 256
    ONESCOL = pool.at(o, "onescol", [128, 2], F32); o += 32
    HBC3 = pool.at(o, "hbc3", [128, NFC, 3], F32); o += 416
    STATS = pool.at(o, "stats", [128, 16], F32); o += 64
    SUMSQ = pool.at(o, "sumsq", [128, 8], F32); o += 32
    RSTDS = pool.at(o, "rstds", [128, 8], F32); o += 32
    ACCS = pool.at(o, "accs", [128, 4], F32); o += 32
    SROW = pool.at(o, "srow", [1, 16], F32); o += 64
    SC = [None] * 3
    SC[0] = pool.at(o, "sc1", [128, 4], F32); o += 32
    SC[1] = pool.at(o, "sc3", [128, 4], F32); o += 32
    SC[2] = pool.at(o, "sc2", [128, 4], F32); o += 32
    SMALL0 = pool.at(o, "small0", [128, 512], F32); o += 2048
    SMALL1 = pool.at(o, "small1", [128, 512], F32); o += 2048
    ZT = pool.at(o, "zt", [NH, 128], F32); o += 512
    DTT = pool.at(o, "dtt", [NH, 128], F32); o += 512
    LDTT = pool.at(o, "ldtt", [NH, 128], F32); o += 512
    ACUM = pool.at(o, "acum", [NH, 128], F32); o += 512
    NAL = pool.at(o, "nal", [NH, 128], F32); o += 512
    E2T = pool.at(o, "e2t", [NH, 128], F32); o += 512
    DTEB = pool.at(o, "dteb", [NH, 128], BF16); o += 256
    EAB = pool.at(o, "eab", [NH, 128], BF16); o += 256
    CDREP = pool.at(o, "cdrep", [NH, 128], BF16); o += 256
    CDCOL = pool.at(o, "cdcol", [NH, 1], F32); o += 32
    SDCOL = pool.at(o, "sdcol", [128, 1], F32); o += 32
    GQ = pool.at(o, "gq", [128, 64], F32); o += 256
    RED = pool.at(o, "red", [128, 128], F32); o += 512
    CEPS = pool.at(o, "ceps", [128, 1], F32); o += 32
    CONE = pool.at(o, "cone", [128, 1], F32); o += 32
    o = (o + 31) & ~31
    o = (o + 511) & ~511
    R1 = o; o += 32768
    R2 = o; o += 35328
    R3 = o; o += 65536
    TOTAL = o

    xs = pool.at(R1, "xs", [128, NCK, 2048], BF16)
    YT0 = pool.at(R1, "yt0", [128, 16, T], BF16)
    Y = pool.at(R3, "Y", [128, NCK, I], BF16)
    SUM = pool.at(R3, "SUMb", [128, 16, T], F32)
    STG = [pool.at(R2, "stgA", [128, D], F32), pool.at(R2 + 8192, "stgB", [128, D], F32)]
    WT = [pool.at(R2 + 16384, "wtA", [128, 16, 128], BF16),
          pool.at(R2 + 20480, "wtB", [128, 16, 128], BF16)]
    CONVIN = [pool.at(R2 + 24576, "convinA", [128, 1027 + 1], F32),
              pool.at(R2 + 28704, "convinB", [128, 1027 + 1], F32)]
    CONVO = pool.at(R2 + 32832, "convo", [128, T], BF16)
    SQS = pool.at(R2 + 24576, "sqs", [128, D], F32)
    XNT = pool.at(R2 + 16384, "xnt", [128, D], BF16)
    FL_L = [pool.at(R2, "flL0", [2, 1024], F32), pool.at(R2 + 4096, "flL1", [2, 1024], F32)]
    FL_R = [pool.at(R2 + 8192, "flR0", [2, 1024], F32), pool.at(R2 + 12288, "flR1", [2, 1024], F32)]
    EMIN = pool.at(R2 + 16384, "emin", [128, 1024], F32)
    S16 = pool.at(R2 + 20480, "s16", [128, 1024], BF16)
    XD32 = pool.at(R2 + 22528, "xd32", [128, 2048], BF16)
    EAX32 = pool.at(R2 + 26624, "eax32", [128, 2048], BF16)
    TS1 = pool.at(R2 + 30720, "ts1", [128, 512], F32)
    TS2 = pool.at(R2 + 32768, "ts2", [128, 512], F32)
    CBTM = pool.at(R2 + 34816, "cbtm", [128, 128], BF16)
    BTOK = pool.at(R2 + 35072, "btok", [128, 128], BF16)
    DREPB = pool.at(R2, "drepb", [1, I], BF16)
    WTG = pool.at(R2 + 16384, "wtg", [128, 16, 512], BF16)
    YT1 = pool.at(R2, "yt1", [128, 16, T], BF16)
    WQ = [pool.at(O_DEXPT, "wqA", [128, 1024], F32), pool.at(O_DEXPT + 4096, "wqB", [128, 1024], F32)]
    WTO = pool.at(O_ST, "wto", [128, 32, 128], BF16)
    G = pool.at(R1, "g", [128, 22, T], BF16)
    FST = [pool.at(R1 + 45056, "fstA", [128, D], F32),
           pool.at(R1 + 53248, "fstB", [128, D], F32)]
    WQ1 = pool.at(O_DEXPT, "wtq1", [128, 16, 128], BF16)
    WQ3 = pool.at(O_DEXPT + 4096, "wtq3", [128, 16, 128], BF16)
    QC1 = pool.at(O_ST, "qc1", [128, D], F32)
    TNEG = pool.at(O_SEL, "tneg", [128, D], BF16)
    W2ST = [pool.at(R1 + 45056, "w2stA", [128, 2784], F32),
            pool.at(R1 + 56192, "w2stB", [128, 2784], F32)]
    WQ2 = pool.at(O_DEXPT, "wtq2", [128, 22, 128], BF16)
    QC1W2 = pool.at(O_ST, "qc1w2", [128, 1408], F32)
    TNEG2 = pool.at(O_DTRAW, "tneg2", [128, 2784], BF16)
    OSTG = [pool.at(R2, "ostgA", [128, D], F32), pool.at(R2 + 8192, "ostgB", [128, D], F32)]
    XRD = [pool.at(R2 + 16384, "xrdA", [128, D], F32), pool.at(R2 + 24576, "xrdB", [128, D], F32)]

    assert TOTAL <= 208896, TOTAL

    PAPS = ctx.enter_context(nc.psum_tensor("PA", [128, 1024], F32))
    PBPS = ctx.enter_context(nc.psum_tensor("PB", [128, 1024], F32))
    PCPS = ctx.enter_context(nc.psum_tensor("PC", [128, 1024], F32))
    PDPS = ctx.enter_context(nc.psum_tensor("PD", [128, 1024], F32))
    PAb = PAPS[:].bitcast(BF16)
    PBb = PBPS[:].bitcast(BF16)
    PCb = PCPS[:].bitcast(BF16)
    PDb = PDPS[:].bitcast(BF16)
    PA, PB, PC, PD = PAPS[:], PBPS[:], PCPS[:], PDPS[:]

    LS = Sem(nc, ctx, "LS")
    LA = Sem(nc, ctx, "LA")
    PE = Sem(nc, ctx, "PE")
    DV = Sem(nc, ctx, "DV")
    AC = Sem(nc, ctx, "AC")
    ST = Sem(nc, ctx, "ST")

    pool.finalize(ctx)
    nc.const_aps.register(F32, float(EPS), CEPS[:])
    nc.const_aps.register(F32, 1.0, CONE[:])
    P = Prog(nc)
    SEMS = (LS, LA, PE, DV, AC)

    def join(engs=("sync", "scalar", "vector", "tensor")):
        for e in engs:
            for s in SEMS:
                P.wait(e, s, s.n)

    def sdma(dst, src, waits=(), inc=True):
        return P.do("sync", lambda e, d=dst, s=src: e.dma_start(out=d, in_=s),
                    waits=waits, inc=(LS, 16) if inc else None)

    def adma(dst, src, waits=(), inc=True):
        return P.do("scalar", lambda e, d=dst, s=src: e.dma_start(out=d, in_=s),
                    waits=waits, inc=(LA, 16) if inc else None)

    def pe_mm(out, lhsT, rhs, start=True, stop=True, waits=(), inc=None):
        return P.do("tensor",
                    lambda e, o_=out, l=lhsT, r=rhs, st=start, sp=stop:
                    nc.tensor.matmul(o_, l, r, start=st, stop=sp),
                    waits=waits, inc=inc)

    def pe_tr(out, in_, ident, waits=(), inc=None):
        return P.do("tensor",
                    lambda e, o_=out, i=in_, d=ident: nc.tensor.transpose(o_, i, d),
                    waits=waits, inc=inc)

    def dv(fn, waits=(), inc=None):
        return P.do("vector", fn, waits=waits, inc=inc)

    def ac(fn, waits=(), inc=None):
        return P.do("scalar", fn, waits=waits, inc=inc)

    for dst, src in [(IDF[:], idf_d), (IDB[:], idb_d), (M1[:], m1_d), (SEL32[:], sel_d),
                     (CWT[:], conv_wt), (CBT[:], conv_bt), (NWT[:], norm_wt_d),
                     (DTB[:], dt_bias), (ALOG[:], a_log), (MSK[:], maskd)]:
        sdma(dst, src)
    c_lds = LS.n
    for buf, val in [(ONES64[:], 1.0), (ZERO64[:], 0.0), (ONESROW[:], 1.0),
                     (ONESROWB[:], 1.0), (ONESCOL[:], 1.0), (RED[:], 0.0),
                     (CEPS[:], float(EPS)), (CONE[:], 1.0)]:
        dv(lambda e, b=buf, v=val: nc.vector.memset(b, v))
    dv(lambda e: nc.vector.tensor_copy(SMALL1[:, 7:8], IDF[:, 0:1]), waits=[(LS, c_lds)])
    c_ms = dv(lambda e: nc.vector.tensor_copy(SMALL1[:, 7:8], IDF[:, 0:1]), inc=(DV, 1))
    c_ae = ac(lambda e: nc.scalar.activation(AEXP[:], ALOG[:], AF.Exp),
              waits=[(LS, c_lds)], inc=(AC, 1))
    adma(XRD[0][0:1, :], d_rep[0:1, 0:2048])
    c_dr = adma(XRD[1][0:1, :], d_rep[0:1, 2048:4096])
    dv(lambda e: nc.vector.tensor_copy(DREPB[0:1, 0:2048], XRD[0][0:1, :]), waits=[(LA, c_dr)])
    c_db = dv(lambda e: nc.vector.tensor_copy(DREPB[0:1, 2048:4096], XRD[1][0:1, :]), inc=(DV, 1))
    for j in range(8):
        pq = PA if j % 2 == 0 else PB
        c_m = pe_mm(pq[:, 0:512], ONESROWB[:], DREPB[0:1, j * 512:(j + 1) * 512],
                    waits=[(DV, c_db if j < 2 else 0), (DV, DV.n if j >= 2 else 0), (LS, c_lds)],
                    inc=(PE, 1))
        dv(lambda e, j=j, pq=pq: nc.vector.tensor_copy(DexpT[:, j * 512:(j + 1) * 512], pq[:, 0:512]),
           waits=[(PE, c_m)], inc=(DV, 1))
    join()

    def ln_phase(xsrc):
        dmac = []
        c_rq = 0
        for i in range(8):
            w = []
            if len(dmac) >= 2:
                w = [(DV, dmac[i - 2][0]), (AC, dmac[i - 2][1])]
            c_x = adma(STG[i % 2][:], xsrc[i * 128:(i + 1) * 128, :], waits=w)
            xt = STG[i % 2][:]
            dv(lambda e, xt=xt, i=i: nc.vector.tensor_reduce(
                SUMSQ[:, i:i + 1], xt, axis=AX.X, op=OP.add), waits=[(LA, c_x)])
            c_sq = ac(lambda e, xt=xt: nc.scalar.activation(
                SQS[:], xt, AF.Square), waits=[(LA, c_x), (DV, c_rq)], inc=(AC, 1))
            c_rq = dv(lambda e, i=i: nc.vector.tensor_reduce(
                RSTDS[:, i:i + 1], SQS[:], axis=AX.X, op=OP.add),
                waits=[(AC, c_sq)], inc=(DV, 1))
            dmac.append((c_rq, c_sq))
        dv(lambda e: nc.vector.tensor_copy(SMALL1[:, 7:8], IDF[:, 0:1]))
        dv(lambda e: nc.vector.tensor_copy(SMALL1[:, 7:8], IDF[:, 0:1]), inc=(DV, 1))
        for i in range(8):
            dv(lambda e, i=i: nc.vector.tensor_scalar(
                STATS[:, i:i + 1], SUMSQ[:, i:i + 1], 1.0 / D, None, OP.mult))
        for i in range(8):
            dv(lambda e, i=i: nc.vector.tensor_tensor(
                SMALL0[:, i:i + 1], STATS[:, i:i + 1], STATS[:, i:i + 1], OP.mult))
        for i in range(8):
            dv(lambda e, i=i: nc.vector.tensor_scalar(
                SMALL1[:, i:i + 1], RSTDS[:, i:i + 1], 1.0 / D, None, OP.mult))
        c_v = 0
        for i in range(8):
            c_v = dv(lambda e, i=i: nc.vector.tensor_tensor(
                SMALL0[:, i:i + 1], SMALL1[:, i:i + 1], SMALL0[:, i:i + 1], OP.subtract),
                inc=(DV, 1) if i == 7 else None) or c_v
        c_sd = ac(lambda e: nc.scalar.activation(
            SMALL1[:, 0:8], SMALL0[:, 0:8], AF.Sqrt, bias=EPS),
            waits=[(DV, c_v)], inc=(AC, 1))
        for i in range(8):
            dv(lambda e, i=i: nc.vector.reciprocal(
                STATS[:, 8 + i:9 + i], SMALL1[:, i:i + 1]),
                waits=[(AC, c_sd)] if i == 0 else ())
        dv(lambda e: nc.vector.tensor_copy(SMALL1[:, 7:8], IDF[:, 0:1]))
        dv(lambda e: nc.vector.tensor_copy(SMALL1[:, 7:8], IDF[:, 0:1]), inc=(DV, 1))
        join()
        prev = [None, None]
        for i in range(8):
            w = []
            if prev[i % 2] is not None:
                w = [(AC, prev[i % 2])]
            c_x = adma(STG[i % 2][:], xsrc[i * 128:(i + 1) * 128, :], waits=w)
            c_n = dv(lambda e, i=i: nc.vector.tensor_scalar(
                XNT[:], STG[i % 2][:], STATS[:, i:i + 1], STATS[:, 8 + i:9 + i],
                OP.subtract, OP.mult), waits=[(LA, c_x), (DV, DV.n), (PE, PE.n)], inc=(DV, 1))
            c_t = None
            for k in range(16):
                pq = PCb if k < 8 else PDb
                c_t = pe_tr(pq[:, (k % 8) * 128:(k % 8 + 1) * 128], XNT[:, k * 128:(k + 1) * 128],
                            IDB[:], waits=[(DV, c_n)] if k == 0 else (),
                            inc=(PE, 1) if k == 15 else None)
            ac(lambda e, i=i: nc.scalar.copy(
                xn_fm[:, 0:8, i * 128:(i + 1) * 128],
                PCb[:, 0:1024].rearrange("p (a b) -> p a b", a=8)), waits=[(PE, c_t)])
            prev[i % 2] = ac(lambda e, i=i: nc.scalar.copy(
                xn_fm[:, 8:16, i * 128:(i + 1) * 128],
                PDb[:, 0:1024].rearrange("p (a b) -> p a b", a=8)), inc=(AC, 1))
        join()

    def proj_run(tiles, halo):
        if halo:
            dv(lambda e: nc.vector.memset(CONVIN[0][:, 0:3], 0.0))
            dv(lambda e: nc.vector.memset(CONVIN[1][:, 0:3], 0.0))
        dv(lambda e: nc.vector.tensor_copy(SMALL1[:, 7:8], IDF[:, 0:1]))
        c_pro = dv(lambda e: nc.vector.tensor_copy(SMALL1[:, 7:8], IDF[:, 0:1]), inc=(DV, 1))
        st_dma = {}
        st_trw = {}
        st_wt = {}
        st_mm = {}
        st_cin = {}
        st_conv = {}
        st_silu = {}
        st_xse = {}
        n = len(tiles)

        def emit_stage(j):
            t = tiles[j]
            rows = t["rows"]
            w = [(PE, st_trw[j - 2])] if j - 2 in st_trw else [(DV, c_pro)]
            st_dma[j] = sdma(STG[j % 2][0:rows, :], w_in[t["row0"]:t["row0"] + rows, :], waits=w)
            w = [(LS, st_dma[j])]
            if j - 1 in st_wt:
                w.append((AC, st_wt[j - 1]))
            if j - 1 in st_mm:
                w.append((PE, st_mm[j - 1]))
            for k in range(16):
                pq = PA if k < 8 else PB
                st_trw[j] = pe_tr(
                    pq[:, (k % 8) * 128:(k % 8) * 128 + rows],
                    STG[j % 2][0:rows, k * 128:(k + 1) * 128], IDF[0:rows, 0:rows],
                    waits=w if k == 0 else (), inc=(PE, 1) if k == 15 else None) or st_trw.get(j)
            ac(lambda e, j=j: nc.scalar.copy(
                WT[j % 2][:, 0:8, :], PA.rearrange("p (a b) -> p a b", a=8)),
                waits=[(PE, st_trw[j])])
            st_wt[j] = ac(lambda e, j=j: nc.scalar.copy(
                WT[j % 2][:, 8:16, :], PB.rearrange("p (a b) -> p a b", a=8)), inc=(AC, 1))

        PIPE = False
        if PIPE:
            emit_stage(0)
        for j, t in enumerate(tiles):
            rows = t["rows"]
            if PIPE and j + 1 < n:
                emit_stage(j + 1)
            elif not PIPE:
                emit_stage(j)
            w = [(AC, st_wt[j])]
            if j - 1 in st_cin:
                w.append((AC, st_cin[j - 1]))
            for nn_ in range(2):
                for k in range(16):
                    st_mm[j] = pe_mm(
                        PC[0:rows, nn_ * 512:(nn_ + 1) * 512], WT[j % 2][:, k, 0:rows],
                        xn_fm[:, k, nn_ * 512:(nn_ + 1) * 512],
                        start=(k == 0), stop=(k == 15),
                        waits=w if (nn_ == 0 and k == 0) else (),
                        inc=(PE, 1) if (nn_ == 1 and k == 15) else None) or st_mm.get(j)
            kind = t["kind"]
            if kind == "dt":
                ac(lambda e: nc.scalar.copy(dt_raw[:, 0:512], PC[0:NH, 0:512]),
                   waits=[(PE, st_mm[j])])
                st_cin[j] = ac(lambda e: nc.scalar.copy(
                    dt_raw[:, 512:1024], PC[0:NH, 512:1024]), inc=(AC, 1))
                continue
            fc = t["fc"]
            cb = j % 2
            w = [(PE, st_mm[j])]
            if j - 2 in st_conv:
                w.append((DV, st_conv[j - 2]))
            ac(lambda e, cb=cb: nc.scalar.copy(CONVIN[cb][:, 3:515], PC[:, 0:512]), waits=w)
            st_cin[j] = ac(lambda e, cb=cb: nc.scalar.copy(
                CONVIN[cb][:, 515:1027], PC[:, 512:1024]), inc=(AC, 1))
            c_front = None
            if not halo:
                c_front = dv(lambda e, cb=cb, fc=fc: nc.vector.tensor_copy(
                    CONVIN[cb][:, 0:3], HBC3[:, fc, :]), waits=[(AC, st_cin[j])], inc=(DV, 1))
            if halo:
                dv(lambda e, cb=cb, fc=fc: nc.vector.tensor_scalar(
                    HBC3[:, fc, :], CONVIN[cb][:, 1024:1027], MSK[:, 0:1], None, OP.mult),
                    waits=[(AC, st_cin[j])])
            w_acc = [(AC, st_silu[j - 1])] if j - 1 in st_silu else []
            for hh in range(2):
                base = hh * 512
                if hh == 0:
                    w0 = [(AC, st_cin[j])] + w_acc
                    if c_front is not None:
                        w0.append((DV, c_front))
                else:
                    w0 = [(AC, st_silu[j])]
                dv(lambda e, cb=cb, fc=fc, base=base: nc.vector.tensor_scalar(
                    PD[:, 0:512], CONVIN[cb][:, base + 0:base + 512], CWT[:, fc, 0:1], None, OP.mult),
                    waits=w0)
                for kk in range(1, 4):
                    st_conv[j] = dv(lambda e, cb=cb, fc=fc, base=base, kk=kk: nc.vector.scalar_tensor_tensor(
                        PD[:, 0:512], CONVIN[cb][:, base + kk:base + kk + 512],
                        CWT[:, fc, kk:kk + 1], PD[:, 0:512], OP.mult, OP.add),
                        inc=(DV, 1) if kk == 3 else None) or st_conv.get(j)
                if kind == "b":
                    dst = B_fm[:, base:base + 512]
                elif kind == "c":
                    dst = C_fm[:, base:base + 512]
                else:
                    dst = CONVO[:, base:base + 512]
                st_silu[j] = ac(lambda e, dst=dst, fc=fc: nc.scalar.activation(
                    dst, PD[:, 0:512], AF.Silu, bias=CBT[:, fc:fc + 1]),
                    waits=[(DV, st_conv[j])], inc=(AC, 1))
            if kind == "xs":
                fcl = t["fcl"]
                w = [(AC, st_silu[j])]
                if j - 1 in st_xse:
                    w.append((DV, st_xse[j - 1]))
                for tck in range(8):
                    c_xt = pe_tr(PDb[:, 1024 + tck * 128:1024 + (tck + 1) * 128],
                                 CONVO[:, tck * 128:(tck + 1) * 128], IDB[:],
                                 waits=w if tck == 0 else (),
                                 inc=(PE, 1) if tck == 7 else None)
                st_xse[j] = dv(lambda e, fcl=fcl: nc.vector.tensor_copy(
                    xs[:, :, fcl * 128:(fcl + 1) * 128],
                    PDb[:, 1024:2048].rearrange("p (a b) -> p a b", a=8)),
                    waits=[(PE, c_xt)], inc=(DV, 1))
        join()

    def bcdt_phase(halo):
        tiles = [
            dict(row0=4096 + 32 * 128, rows=128, kind="b", fc=32),
            dict(row0=4096 + 33 * 128, rows=128, kind="c", fc=33),
            dict(row0=8448, rows=64, kind="dt"),
        ]
        proj_run(tiles, halo)

    def xs_phase(halo, half):
        import os as _os
        nt = int(_os.environ.get("XSLIM", "16"))
        kk = _os.environ.get("XSKIND", "xs")
        tiles = [dict(row0=4096 + (half * 16 + f) * 128, rows=128, kind=kk,
                      fc=half * 16 + f, fcl=f)
                 for f in range(nt)]
        proj_run(tiles, halo)

    def chain_phase(halo, half):
        own = not halo
        if own:
            for fl in (FL_L[0], FL_L[1], FL_R[0], FL_R[1]):
                adma(fl[0:2, :], onesd[0:2, :])
        dv(lambda e: nc.vector.tensor_copy(SMALL1[:, 7:8], IDF[:, 0:1]))
        dv(lambda e: nc.vector.tensor_copy(SMALL1[:, 7:8], IDF[:, 0:1]), inc=(DV, 1))
        join()
        st = STATES[half]
        stv = st[:].rearrange("p a b -> p (a b)")
        if halo:
            adma(stv, zerosd[:, :])
        pqs = [PA, PB, PC, PD]
        c_paircons = {0: 0, 1: 0}
        c_quadcons = {0: 0, 1: 0}
        c_flat = {0: 0, 1: 0}
        c_flatuse = {0: 0, 1: 0}
        c_flatprev = 0
        c_cs_prev = 0
        c_btok = 0
        c_pe_prev = 0
        c11_prev = 0
        c_exp = c_yd = c_yp = c_cs = c_cdx = 0
        for ck in range(NCK):
            cks = slice(ck * 128, (ck + 1) * 128)
            c1 = dv(lambda e, cks=cks: nc.vector.tensor_scalar(
                ZT[:], dt_raw[:, cks], DTB[:, 0:1], None, OP.add),
                waits=[(PE, c_pe_prev)], inc=(DV, 1))
            ca = ac(lambda e: nc.scalar.activation(E2T[:], ZT[:], AF.Abs),
                    waits=[(DV, c1), (PE, c_pe_prev), (DV, c11_prev)], inc=(AC, 1))
            P.wait("scalar", AC, ca)
            ca = ac(lambda e: nc.scalar.activation(LDTT[:], E2T[:], AF.Exp, scale=-1.0),
                    inc=(AC, 1))
            P.wait("scalar", AC, ca)
            ca = ac(lambda e: nc.scalar.activation(E2T[:], LDTT[:], AF.Ln, bias=1.0),
                    inc=(AC, 1))
            P.wait("scalar", AC, ca)
            c2 = ac(lambda e: nc.scalar.activation(LDTT[:], ZT[:], AF.Relu), inc=(AC, 1))
            c3 = dv(lambda e: nc.vector.tensor_tensor(DTT[:], LDTT[:], E2T[:], OP.add),
                    waits=[(AC, c2)], inc=(DV, 1))
            dv(lambda e: nc.vector.tensor_scalar(
                ZT[:], DTT[:], AEXP[:, 0:1], -1.0, OP.mult, OP.mult))
            c5 = dv(lambda e: nc.vector.tensor_tensor_scan(
                ACUM[:], ZT[:], ZERO64[:], 0.0, OP.add, OP.add),
                waits=[(LA, c_flatprev)], inc=(DV, 1))
            c7 = c5
            if own:
                c6 = ac(lambda e: nc.scalar.activation(LDTT[:], DTT[:], AF.Ln),
                        waits=[(DV, c5)], inc=(AC, 1))
                c7 = dv(lambda e: nc.vector.tensor_tensor(NAL[:], LDTT[:], ACUM[:], OP.subtract),
                        waits=[(AC, c6), (LA, c_flatprev)], inc=(DV, 1))
            ac(lambda e: nc.scalar.activation(
                E2T[:], ACUM[:], AF.Exp, scale=-1.0, bias=ACUM[:, 127:128]),
                waits=[(DV, c7)])
            if own:
                ac(lambda e: nc.scalar.activation(EAB[:], ACUM[:], AF.Exp))
            c9 = ac(lambda e: nc.scalar.activation(CDCOL[:], ACUM[:, 127:128], AF.Exp),
                    inc=(AC, 1))
            dv(lambda e: nc.vector.tensor_tensor(DTEB[:], E2T[:], DTT[:], OP.mult),
               waits=[(AC, c9)])
            dv(lambda e: nc.vector.tensor_scalar(
                CDREP[:], ONES64[:], CDCOL[:, 0:1], None, OP.mult))
            dv(lambda e: nc.vector.tensor_copy(SMALL1[:, 7:8], IDF[:, 0:1]))
            c11 = dv(lambda e: nc.vector.tensor_copy(SMALL1[:, 7:8], IDF[:, 0:1]), inc=(DV, 1))
            c11_prev = c11
            c_xd = 0
            for jj in range(4):
                wv = [(DV, c11),
                      (DV, c_paircons[jj % 2]) if jj < 2 else (DV, c_quadcons[jj % 2])]
                c_m = pe_mm(pqs[jj][:, 0:512], DTEB[half * 32:half * 32 + 32, :],
                            SEL32[half * 32:half * 32 + 32, jj * 512:(jj + 1) * 512],
                            waits=wv, inc=(PE, 1))
                c_xd = dv(lambda e, jj=jj, ck=ck: nc.vector.tensor_tensor(
                    XD32[:, jj * 512:(jj + 1) * 512], xs[:, ck, jj * 512:(jj + 1) * 512],
                    pqs[jj][:, 0:512], OP.mult),
                    waits=[(PE, c_m), (PE, c_cs_prev)], inc=(DV, 1))
            if own:
                c_eax = 0
                for jj in range(4):
                    c_m = pe_mm(pqs[jj][:, 0:512], EAB[half * 32:half * 32 + 32, :],
                                SEL32[half * 32:half * 32 + 32, jj * 512:(jj + 1) * 512],
                                waits=[(DV, c_xd)], inc=(PE, 1))
                    c_eax = ac(lambda e, jj=jj: nc.scalar.copy(
                        EAX32[:, jj * 512:(jj + 1) * 512], pqs[jj][:, 0:512]),
                        waits=[(PE, c_m)], inc=(AC, 1))
                c_m = pe_mm(PC[:, 0:128], B_fm[:, cks], C_fm[:, cks],
                            waits=[(AC, c_eax)], inc=(PE, 1))
                c_cbtm = dv(lambda e: nc.vector.tensor_tensor(
                    CBTM[:], PC[:, 0:128], M1[:], OP.mult), waits=[(PE, c_m)], inc=(DV, 1))
            c_m = pe_tr(PCb[:, 1024:1152], B_fm[:, cks], IDB[:],
                        waits=[(DV, c_cbtm if own else c_xd),
                               (DV, c_quadcons[0])], inc=(PE, 1))
            c_btok = dv(lambda e: nc.vector.tensor_copy(BTOK[:], PCb[:, 1024:1152]),
                        waits=[(PE, c_m), (PE, c_cs_prev)], inc=(DV, 1))
            if own:
                def emit_flats(hg_):
                    rows0 = half * 32 + hg_ * 8
                    adma(FL_L[hg_ % 2][0:1, :], NAL[rows0:rows0 + 8, :],
                         waits=[(DV, c11), (PE, c_flatuse[hg_ % 2])])
                    c_flat[hg_ % 2] = adma(FL_R[hg_ % 2][1:2, :], ACUM[rows0:rows0 + 8, :])
                emit_flats(0)
                emit_flats(1)
                c_flatprev = c_flat[1]
                for hg in range(4):
                    pair = PA if hg % 2 == 0 else PB
                    quad = PC if hg % 2 == 0 else PD
                    wv = [(LA, c_flat[hg % 2]), (DV, c_paircons[hg % 2]), (DV, c_btok)]
                    for i in range(8):
                        col = i * 128
                        c_exp = pe_mm(pair[:, i * 128:(i + 1) * 128],
                                      FL_L[hg % 2][:, col:col + 128],
                                      FL_R[hg % 2][:, col:col + 128],
                                      waits=wv if i == 0 else (),
                                      inc=(PE, 1) if i == 7 else None) or c_exp
                    c_flatuse[hg % 2] = c_exp
                    if hg + 2 < 4:
                        emit_flats(hg + 2)
                    c_em = dv(lambda e, pair=pair: nc.vector.tensor_scalar(
                        EMIN[:], pair[:], 30.0, None, OP.min), waits=[(PE, c_exp)], inc=(DV, 1))
                    c_ex = ac(lambda e: nc.scalar.activation(EMIN[:], EMIN[:], AF.Exp),
                              waits=[(DV, c_em)], inc=(AC, 1))
                    c_s16 = dv(lambda e: nc.vector.tensor_tensor(
                        S16[:].rearrange("p (a b) -> p a b", a=8),
                        EMIN[:].rearrange("p (a b) -> p a b", a=8),
                        CBTM[:].unsqueeze(1).broadcast_to([128, 8, 128]), OP.mult),
                        waits=[(AC, c_ex)], inc=(DV, 1))
                    wv = [(DV, c_s16), (DV, c_quadcons[hg % 2])]
                    for i in range(8):
                        hl = hg * 8 + i
                        c_yd = pe_mm(quad[:, i * 64:(i + 1) * 64], S16[:, i * 128:(i + 1) * 128],
                                     xs[:, ck, hl * 64:(hl + 1) * 64],
                                     waits=wv if i == 0 else (),
                                     inc=(PE, 1) if i == 7 else None) or c_yd
                    for i in range(8):
                        hl = hg * 8 + i
                        c_yp = pe_mm(quad[:, 512 + i * 64:512 + (i + 1) * 64],
                                     C_fm[:, cks], stv[:, hl * 64:(hl + 1) * 64],
                                     inc=(PE, 1) if i == 7 else None) or c_yp
                    for i in range(8):
                        hl = hg * 8 + i
                        c_cs = pe_mm(pair[:, i * 64:(i + 1) * 64], BTOK[:],
                                     XD32[:, hl * 64:(hl + 1) * 64],
                                     waits=[(DV, c_em)] if i == 0 else (),
                                     inc=(PE, 1) if i == 7 else None) or c_cs
                    gofs = half * 2048 + hg * 512
                    dv(lambda e, ck=ck, hg=hg, gofs=gofs: nc.vector.tensor_tensor(
                        TS1[:], xs[:, ck, hg * 512:(hg + 1) * 512],
                        DexpT[:, gofs:gofs + 512], OP.mult))
                    dv(lambda e, quad=quad: nc.vector.tensor_tensor(
                        TS2[:], TS1[:], quad[:, 0:512], OP.add), waits=[(PE, c_yd)])
                    dv(lambda e, quad=quad, hg=hg: nc.vector.tensor_tensor(
                        TS1[:], quad[:, 512:1024], EAX32[:, hg * 512:(hg + 1) * 512], OP.mult),
                        waits=[(PE, c_yp)])
                    c_yw = dv(lambda e, ck=ck, gofs=gofs: nc.vector.tensor_tensor(
                        Y[:, ck, gofs:gofs + 512], TS2[:], TS1[:], OP.add), inc=(DV, 1))
                    c_cdx = pe_mm(quad[:, 0:512], CDREP[half * 32:half * 32 + 32, :],
                                  SEL32[half * 32:half * 32 + 32, hg * 512:(hg + 1) * 512],
                                  waits=[(DV, c_yw)], inc=(PE, 1))
                    sl = stv[:, hg * 512:(hg + 1) * 512]
                    c_quadcons[hg % 2] = dv(lambda e, sl=sl, quad=quad: nc.vector.tensor_tensor(
                        sl, sl, quad[:, 0:512], OP.mult), waits=[(PE, c_cdx)], inc=(DV, 1))
                    c_paircons[hg % 2] = dv(lambda e, sl=sl, pair=pair: nc.vector.tensor_tensor(
                        sl, sl, pair[:, 0:512], OP.add), waits=[(PE, c_cs)], inc=(DV, 1))
                    c_cs_prev = c_cs
                    c_pe_prev = c_cdx
            else:
                for hg in range(4):
                    pair = PA if hg % 2 == 0 else PB
                    quad = PC if hg % 2 == 0 else PD
                    for i in range(8):
                        hl = hg * 8 + i
                        c_cs = pe_mm(pair[:, i * 64:(i + 1) * 64], BTOK[:],
                                     XD32[:, hl * 64:(hl + 1) * 64],
                                     waits=[(DV, c_paircons[hg % 2]), (DV, c_btok)] if i == 0 else (),
                                     inc=(PE, 1) if i == 7 else None) or c_cs
                    c_cdx = pe_mm(quad[:, 0:512], CDREP[half * 32:half * 32 + 32, :],
                                  SEL32[half * 32:half * 32 + 32, hg * 512:(hg + 1) * 512],
                                  waits=[(DV, c_quadcons[hg % 2]), (DV, c11)], inc=(PE, 1))
                    sl = stv[:, hg * 512:(hg + 1) * 512]
                    c_quadcons[hg % 2] = dv(lambda e, sl=sl, quad=quad: nc.vector.tensor_tensor(
                        sl, sl, quad[:, 0:512], OP.mult), waits=[(PE, c_cdx)], inc=(DV, 1))
                    c_paircons[hg % 2] = dv(lambda e, sl=sl, pair=pair: nc.vector.tensor_tensor(
                        sl, sl, pair[:, 0:512], OP.add), waits=[(PE, c_cs)], inc=(DV, 1))
                    c_cs_prev = c_cs
                    c_pe_prev = c_cdx
        if halo:
            dv(lambda e: nc.vector.tensor_scalar(stv, stv, MSK[:, 0:1], None, OP.mult))
        join()

    def gate_phase():
        c_wtg = {0: 0}
        c_mmdone = 0
        c_sil = {0: 0, 1: 0}
        c_dvp = {0: 0, 1: 0}
        c_acprev = 0
        for fg in range(8):
            c_tr = 0
            for s in range(4):
                c_x = sdma(STG[s % 2][:], w_in[fg * 512 + s * 128:fg * 512 + (s + 1) * 128, :],
                           waits=[(PE, c_tr)] if s >= 2 else [(PE, c_mmdone)])
                wv = [(LS, c_x), (AC, c_wtg[0]), (PE, c_mmdone)]
                for k in range(16):
                    pq = PA if k < 8 else PB
                    c_tr = pe_tr(pq[:, (k % 8) * 128:(k % 8) * 128 + 128],
                                 STG[s % 2][:, k * 128:(k + 1) * 128], IDF[:],
                                 waits=wv if k == 0 else (),
                                 inc=(PE, 1) if k == 15 else None) or c_tr
                ac(lambda e, s=s: nc.scalar.copy(
                    WTG[:, 0:8, s * 128:(s + 1) * 128], PA.rearrange("p (a b) -> p a b", a=8)),
                    waits=[(PE, c_tr)])
                c_wtg[0] = ac(lambda e, s=s: nc.scalar.copy(
                    WTG[:, 8:16, s * 128:(s + 1) * 128], PB.rearrange("p (a b) -> p a b", a=8)),
                    inc=(AC, 1))
            for tck in range(8):
                pq = PC if tck % 2 == 0 else PD
                wv = [(AC, c_wtg[0]), (AC, c_sil[tck % 2]), (DV, c_dvp[tck % 2])]
                for k in range(16):
                    c_mmdone = pe_mm(pq[:, 0:512], xn_fm[:, k, tck * 128:(tck + 1) * 128],
                                     WTG[:, k, :], start=(k == 0), stop=(k == 15),
                                     waits=wv if k == 0 else (),
                                     inc=(PE, 1) if k == 15 else None) or c_mmdone
                sb = SMALL0 if tck % 2 == 0 else SMALL1
                c_si = ac(lambda e, pq=pq, sb=sb: nc.scalar.activation(
                    sb[:], pq[:, 0:512], AF.Silu), waits=[(PE, c_mmdone)], inc=(AC, 1))
                c_yg = dv(lambda e, tck=tck, fg=fg, sb=sb: nc.vector.tensor_tensor(
                    Y[:, tck, fg * 512:(fg + 1) * 512], Y[:, tck, fg * 512:(fg + 1) * 512],
                    sb[:], OP.mult), waits=[(AC, c_si)], inc=(DV, 1))
                c_sq = ac(lambda e, tck=tck, fg=fg, pq=pq: nc.scalar.activation(
                    pq[:, 0:512], Y[:, tck, fg * 512:(fg + 1) * 512], AF.Square),
                    waits=[(DV, c_yg), (DV, c_acprev)], inc=(AC, 1))
                c_ac = dv(lambda e, tck=tck, fg=fg, pq=pq: nc.vector.tensor_reduce(
                    GQ[:, fg * 8 + tck:fg * 8 + tck + 1], pq[:, 0:512], axis=AX.X, op=OP.add),
                    waits=[(AC, c_sq)], inc=(DV, 1))
                c_sil[tck % 2] = c_sq
                c_dvp[tck % 2] = c_ac
                c_acprev = c_ac
        join()
        c_sm = 0
        for tck in range(8):
            c_sm = dv(lambda e, tck=tck: nc.vector.tensor_reduce(
                SUMSQ[:, tck:tck + 1], GQ[:, tck::8], axis=AX.X, op=OP.add),
                inc=(DV, 1) if tck == 7 else None) or c_sm
        dv(lambda e: nc.vector.tensor_copy(SMALL1[:, 7:8], IDF[:, 0:1]))
        dv(lambda e: nc.vector.tensor_copy(SMALL1[:, 7:8], IDF[:, 0:1]), inc=(DV, 1))
        c_sd = ac(lambda e: nc.scalar.activation(
            SMALL0[:, 0:8], SUMSQ[:, 0:8], AF.Sqrt, scale=1.0 / I, bias=EPS),
            waits=[(DV, c_sm)], inc=(AC, 1))
        for tck in range(8):
            dv(lambda e, tck=tck: nc.vector.reciprocal(
                RSTDS[:, tck:tck + 1], SMALL0[:, tck:tck + 1]),
                waits=[(AC, c_sd)] if tck == 0 else ())
        c_r = dv(lambda e: nc.vector.tensor_copy(SMALL1[:, 7:8], IDF[:, 0:1]), inc=(DV, 1))
        for tck in range(8):
            dv(lambda e, tck=tck: nc.vector.tensor_scalar(
                Y[:, tck, :], Y[:, tck, :], RSTDS[:, tck:tck + 1], None, OP.mult),
                waits=[(DV, c_r)] if tck == 0 else ())
        dv(lambda e: nc.vector.tensor_copy(SMALL1[:, 7:8], IDF[:, 0:1]), inc=(DV, 1))
        join()

    def try_phase():
        c_ev = {0: 0, 1: 0}
        for ic in range(32):
            yt = YT0 if ic < 16 else YT1
            for tg in range(2):
                par = (ic * 2 + tg) % 2
                pq = PCb if par == 0 else PDb
                pqf = PC if par == 0 else PD
                c_tr = 0
                for j in range(4):
                    c_tr = pe_tr(pq[:, j * 128:(j + 1) * 128],
                                 Y[:, tg * 4 + j, ic * 128:(ic + 1) * 128], IDB[:],
                                 waits=[(DV, c_ev[par])] if j == 0 else (),
                                 inc=(PE, 1) if j == 3 else None) or c_tr
                c_ev[par] = dv(lambda e, yt=yt, ic=ic, tg=tg, pq=pq: nc.vector.tensor_scalar(
                    yt[:, ic % 16, tg * 512:(tg + 1) * 512], pq[:, 0:512],
                    NWT[:, ic:ic + 1], None, OP.mult), waits=[(PE, c_tr)], inc=(DV, 1))
        join()

    def wout_phase():
        c_ev = 0
        c_mm = 0
        c_su = {0: 0, 1: 0}
        for dm in range(16):
            c_tr = 0
            for q in range(4):
                c_x = sdma(WQ[q % 2][:], w_out[dm * 128:(dm + 1) * 128, q * 1024:(q + 1) * 1024],
                           waits=[(PE, c_tr)] if (dm * 4 + q) >= 2 else ())
                pq = PA if q % 2 == 0 else PB
                wv = [(LS, c_x), (AC, c_ev), (PE, c_mm)]
                for k in range(8):
                    c_tr = pe_tr(pq[:, k * 128:(k + 1) * 128],
                                 WQ[q % 2][:, k * 128:(k + 1) * 128], IDF[:],
                                 waits=wv if k == 0 else (),
                                 inc=(PE, 1) if k == 7 else None) or c_tr
                c_ev = ac(lambda e, q=q, pq=pq: nc.scalar.copy(
                    WTO[:, q * 8:(q + 1) * 8, :], pq.rearrange("p (a b) -> p a b", a=8)),
                    waits=[(PE, c_tr)], inc=(AC, 1))
            for ng in range(2):
                pq = PC if ng == 0 else PD
                wv = [(AC, c_ev), (AC, c_su[ng])]
                for ic in range(32):
                    yt = YT0 if ic < 16 else YT1
                    c_mm = pe_mm(pq[:, 0:512], WTO[:, ic, :],
                                 yt[:, ic % 16, ng * 512:(ng + 1) * 512],
                                 start=(ic == 0), stop=(ic == 31),
                                 waits=wv if ic == 0 else (),
                                 inc=(PE, 1) if ic == 31 else None) or c_mm
                c_su[ng] = ac(lambda e, dm=dm, ng=ng, pq=pq: nc.scalar.copy(
                    SUM[:, dm, ng * 512:(ng + 1) * 512], pq[:, 0:512]),
                    waits=[(PE, c_mm)], inc=(AC, 1))
        join()

    def ffn_phase():
        c_red = 0
        c_dma = {}
        w_srcs = [(w1, 43, 0), (w3, 43, 43)]
        for wsrc, ntile, cbase in w_srcs:
            for it in range(ntile):
                rows = 85 if it == 42 else 128
                wv = [(DV, c_red)] if c_red else []
                c_dma[it % 2] = sdma(FST[it % 2][0:rows, :], wsrc[it * 128:it * 128 + rows, :],
                                     waits=wv)
                c_red = dv(lambda e, it=it, rows=rows, cbase=cbase: nc.vector.tensor_reduce(
                    RED[0:rows, cbase + it:cbase + it + 1], FST[it % 2][0:rows, :],
                    axis=AX.X, op=OP.add, apply_absolute_value=True),
                    waits=[(LS, c_dma[it % 2])], inc=(DV, 1))
        for it in range(32):
            dm = it // 2
            hfb = it % 2
            col0 = 0 if hfb == 0 else 2688
            cols = 2688 if hfb == 0 else 2773
            c_dma[it % 2] = sdma(W2ST[it % 2][:, 0:cols],
                                 w2[dm * 128:(dm + 1) * 128, col0:col0 + cols],
                                 waits=[(DV, c_red)])
            c_red = dv(lambda e, it=it, cols=cols: nc.vector.tensor_reduce(
                RED[:, 86 + it:87 + it], W2ST[it % 2][:, 0:cols], axis=AX.X, op=OP.add,
                apply_absolute_value=True), waits=[(LS, c_dma[it % 2])], inc=(DV, 1))
        dv(lambda e: nc.vector.tensor_copy(SMALL1[:, 7:8], IDF[:, 0:1]))
        dv(lambda e: nc.vector.tensor_copy(SMALL1[:, 7:8], IDF[:, 0:1]), inc=(DV, 1))
        for m, (cb, nn) in enumerate(((0, 43), (43, 43), (86, 32))):
            dv(lambda e, m=m, cb=cb, nn=nn: nc.vector.tensor_reduce(
                ACCS[:, m:m + 1], RED[:, cb:cb + nn], axis=AX.X, op=OP.add))
        dv(lambda e: nc.vector.tensor_copy(SMALL1[:, 7:8], IDF[:, 0:1]))
        dv(lambda e: nc.vector.tensor_copy(SMALL1[:, 7:8], IDF[:, 0:1]))
        dv(lambda e: nc.vector.tensor_copy(SMALL1[:, 7:8], IDF[:, 0:1]))
        c_red = dv(lambda e: nc.vector.tensor_copy(SMALL1[:, 7:8], IDF[:, 0:1]), inc=(DV, 1))
        c_d = 0
        for m in range(3):
            c_d = pe_mm(PA[0:1, m * 2:m * 2 + 2], ACCS[:, m:m + 1], ONESCOL[:, 0:2],
                        waits=[(DV, c_red)] if m == 0 else (), inc=(PE, 1))
        dv(lambda e: nc.vector.tensor_copy(SROW[0:1, 0:3], PA[0:1, 0:6:2]),
           waits=[(PE, c_d)])
        dv(lambda e: nc.vector.tensor_copy(SMALL1[:, 7:8], IDF[:, 0:1]))
        dv(lambda e: nc.vector.tensor_copy(SMALL1[:, 7:8], IDF[:, 0:1]))
        c_sr = dv(lambda e: nc.vector.tensor_copy(SMALL1[:, 7:8], IDF[:, 0:1]), inc=(DV, 1))
        c_bc = pe_mm(PB[:, 0:4], ONESROW[:], SROW[0:1, 0:4],
                     waits=[(DV, c_sr)], inc=(PE, 1))
        dv(lambda e: nc.vector.tensor_copy(SMALL0[:, 0:3], PB[:, 0:3]), waits=[(PE, c_bc)])
        dv(lambda e: nc.vector.tensor_copy(SMALL1[:, 7:8], IDF[:, 0:1]))
        dv(lambda e: nc.vector.tensor_copy(SMALL1[:, 7:8], IDF[:, 0:1]))
        for m, cnt in ((0, W1CNT), (1, W1CNT), (2, W2CNT)):
            dv(lambda e, m=m, cnt=cnt: nc.vector.tensor_scalar(
                SC[m][:, 0:1], SMALL0[:, m:m + 1], 0.5 / cnt, None, OP.mult))
            dv(lambda e, m=m, cnt=cnt: nc.vector.tensor_scalar(
                SC[m][:, 2:3], SMALL0[:, m:m + 1], -1.0 / cnt, None, OP.mult))
        for m in range(3):
            dv(lambda e, m=m: nc.vector.tensor_scalar(
                SC[m][:, 1:2], SC[m][:, 0:1], -1.0, None, OP.mult))
        dv(lambda e: nc.vector.tensor_copy(SMALL1[:, 7:8], IDF[:, 0:1]))
        dv(lambda e: nc.vector.tensor_copy(SMALL1[:, 7:8], IDF[:, 0:1]), inc=(DV, 1))
        join()

        c_g = {0: 0, 1: 0}
        c_qtr = 0
        c_wq = 0
        c_mm1 = 0
        c_sumrd = 0
        for hfb in range(2):
            hcs = list(range(0, 21)) if hfb == 0 else list(range(21, 43))
            for hci, hc in enumerate(hcs):
                rows = 85 if hc == 42 else 128
                for im, (wsrc, wq, sci) in enumerate(((w1, WQ1, 0), (w3, WQ3, 1))):
                    c_x = sdma(FST[im][0:rows, :], wsrc[hc * 128:hc * 128 + rows, :],
                               waits=[(PE, c_qtr)])
                    c_c1 = dv(lambda e, im=im, rows=rows, sci=sci: nc.vector.tensor_scalar(
                        QC1[0:rows, :], FST[im][0:rows, :], SC[sci][:, 0:1][0:rows, :],
                        None, OP.is_gt), waits=[(LS, c_x), (PE, c_qtr)])
                    c_tn = dv(lambda e, im=im, rows=rows, sci=sci: nc.vector.scalar_tensor_tensor(
                        TNEG[0:rows, :], FST[im][0:rows, :], SC[sci][:, 1:2][0:rows, :],
                        QC1[0:rows, :], OP.is_lt, OP.subtract), inc=(DV, 1))
                    wv = [(DV, c_tn), (AC, c_wq), (PE, c_mm1)]
                    for k in range(16):
                        pq = PAb if k < 8 else PBb
                        c_qtr = pe_tr(pq[:, (k % 8) * 128:(k % 8) * 128 + rows],
                                      TNEG[0:rows, k * 128:(k + 1) * 128], IDB[0:rows, 0:rows],
                                      waits=wv if k == 0 else (),
                                      inc=(PE, 1) if k == 15 else None) or c_qtr
                    ac(lambda e, wq=wq: nc.scalar.copy(
                        wq[:, 0:8, :], PAb[:, 0:1024].rearrange("p (a b) -> p a b", a=8)),
                        waits=[(PE, c_qtr)])
                    c_wq = ac(lambda e, wq=wq: nc.scalar.copy(
                        wq[:, 8:16, :], PBb[:, 0:1024].rearrange("p (a b) -> p a b", a=8)),
                        inc=(AC, 1))
                for ng in range(2):
                    wv = [(AC, c_wq), (AC, c_g[0]), (DV, c_g[1])]
                    for k in range(16):
                        pe_mm(PC[0:rows, 0:512], WQ1[:, k, 0:rows],
                              xn_fm[:, k, ng * 512:(ng + 1) * 512],
                              start=(k == 0), stop=(k == 15),
                              waits=wv if k == 0 else ())
                    for k in range(16):
                        c_mm1 = pe_mm(PC[0:rows, 512:1024], WQ3[:, k, 0:rows],
                                      xn_fm[:, k, ng * 512:(ng + 1) * 512],
                                      start=(k == 0), stop=(k == 15),
                                      inc=(PE, 1) if k == 15 else None) or c_mm1
                    c_g[0] = ac(lambda e, rows=rows: nc.scalar.activation(
                        SMALL0[0:rows, :], PC[0:rows, 0:512], AF.Silu,
                        scale=SC[0][0:rows, 2:3]), waits=[(PE, c_mm1)], inc=(AC, 1))
                    c_g[1] = dv(lambda e, hci=hci, ng=ng, rows=rows: nc.vector.scalar_tensor_tensor(
                        G[0:rows, hci, ng * 512:(ng + 1) * 512], PC[0:rows, 512:1024],
                        SC[1][0:rows, 2:3], SMALL0[0:rows, :], OP.mult, OP.mult),
                        waits=[(AC, c_g[0])], inc=(DV, 1))
            col0 = 0 if hfb == 0 else 2688
            cols = 2688 if hfb == 0 else 2773
            nkc = len(hcs)
            c_w2tr = 0
            c_w2wq = 0
            c_w2mm = 0
            for dm in range(16):
                c_x = sdma(W2ST[dm % 2][:, 0:cols], w2[dm * 128:(dm + 1) * 128, col0:col0 + cols],
                           waits=[(PE, c_w2tr)] if dm >= 2 else [(DV, c_g[1]), (PE, c_qtr), (PE, c_mm1)])
                nch = (cols + 1407) // 1408
                c_tn = 0
                for ch in range(nch):
                    c0 = ch * 1408
                    c1_ = min(cols, c0 + 1408)
                    dv(lambda e, dm=dm, c0=c0, c1_=c1_: nc.vector.tensor_scalar(
                        QC1W2[:, 0:c1_ - c0], W2ST[dm % 2][:, c0:c1_], SC[2][:, 0:1],
                        None, OP.is_gt), waits=[(LS, c_x), (PE, c_w2tr)] if ch == 0 else ())
                    c_tn = dv(lambda e, dm=dm, c0=c0, c1_=c1_: nc.vector.scalar_tensor_tensor(
                        TNEG2[:, c0:c1_], W2ST[dm % 2][:, c0:c1_], SC[2][:, 1:2],
                        QC1W2[:, 0:c1_ - c0], OP.is_lt, OP.subtract), inc=(DV, 1))
                wv = [(DV, c_tn), (AC, c_w2wq), (PE, c_w2mm)]
                for ki, hc in enumerate(hcs):
                    rows = 85 if hc == 42 else 128
                    pq = PAb if ki % 2 == 0 else PBb
                    c_w2tr = pe_tr(pq[0:rows, 0:128], TNEG2[:, ki * 128:ki * 128 + rows], IDB[:],
                                   waits=(wv + [(AC, c_w2wq)]) if ki == 0 else [(AC, c_w2wq)],
                                   inc=(PE, 1))
                    c_w2wq = ac(lambda e, ki=ki, pq=pq, rows=rows: nc.scalar.copy(
                        WQ2[0:rows, ki, :], pq[0:rows, 0:128]),
                        waits=[(PE, c_w2tr)], inc=(AC, 1))
                for ng in range(2):
                    pq = PC if ng == 0 else PD
                    wv = [(AC, c_w2wq), (DV, c_sumrd)]
                    for ki, hc in enumerate(hcs):
                        rows = 85 if hc == 42 else 128
                        c_w2mm = pe_mm(pq[:, 0:512], WQ2[0:rows, ki, :],
                                       G[0:rows, ki, ng * 512:(ng + 1) * 512],
                                       start=(ki == 0), stop=(ki == nkc - 1),
                                       waits=wv if ki == 0 else (),
                                       inc=(PE, 1) if ki == nkc - 1 else None) or c_w2mm
                    c_sumrd = dv(lambda e, dm=dm, ng=ng, pq=pq: nc.vector.scalar_tensor_tensor(
                        SUM[:, dm, ng * 512:(ng + 1) * 512], pq[:, 0:512], SC[2][:, 2:3],
                        SUM[:, dm, ng * 512:(ng + 1) * 512], OP.mult, OP.add),
                        waits=[(PE, c_w2mm)], inc=(DV, 1))
            join()
            if hfb == 1 and "sc" in dbg_outs:
                for mi in range(3):
                    P.do("sync", lambda e, mi=mi: e.dma_start(
                        out=dbg_outs["sc"][:, mi * 4:mi * 4 + 4], in_=SC[mi][:]), inc=(ST, 16))
                P.wait("sync", ST, ST.n)
            if hfb == 1:
                dbg_dump("g", G[:])

    def out_phase():
        c_ost = {0: 0, 1: 0}
        c_ttp = {0: 0, 1: 0}
        for tck in range(8):
            wv = [(DV, c_ttp[tck % 2])] if tck >= 2 else ()
            c_x = adma(XRD[tck % 2][:], xo[tck * 128:(tck + 1) * 128, :], waits=wv)
            c_tt = 0
            for b4 in range(4):
                pq = PA if b4 % 2 == 0 else PB
                c_tr = 0
                for j in range(4):
                    c_tr = pe_tr(pq[:, j * 128:(j + 1) * 128],
                                 SUM[:, b4 * 4 + j, tck * 128:(tck + 1) * 128], IDF[:],
                                 waits=[(DV, c_tt)] if j == 0 else (),
                                 inc=(PE, 1) if j == 3 else None) or c_tr
                c_tt = dv(lambda e, tck=tck, b4=b4, pq=pq: nc.vector.tensor_tensor(
                    OSTG[tck % 2][:, b4 * 512:(b4 + 1) * 512], pq[:, 0:512],
                    XRD[tck % 2][:, b4 * 512:(b4 + 1) * 512], OP.add),
                    waits=[(PE, c_tr), (LA, c_x), (ST, c_ost[tck % 2])], inc=(DV, 1))
            dv(lambda e: nc.vector.tensor_copy(SMALL1[:, 7:8], IDF[:, 0:1]))
            c_tt = dv(lambda e: nc.vector.tensor_copy(SMALL1[:, 7:8], IDF[:, 0:1]), inc=(DV, 1))
            c_ttp[tck % 2] = c_tt
            c_ost[tck % 2] = P.do(
                "sync", lambda e, tck=tck: e.dma_start(
                    out=out_d[tck * 128:(tck + 1) * 128, :], in_=OSTG[tck % 2][:]),
                waits=[(DV, c_tt)], inc=(ST, 16))
        P.wait("sync", ST, ST.n)

    def dbg_dump(name, src_ap):
        if name in dbg_outs:
            join()
            P.do("sync", lambda e, d=dbg_outs[name], s=src_ap: e.dma_start(out=d, in_=s),
                 inc=(ST, 16))
            P.wait("sync", ST, ST.n)

    PH = [0]
    def go():
        PH[0] += 1
        return PH[0] <= stopat
    if nohalo:
        adma(STATES[0][:].rearrange("p a b -> p (a b)"), zerosd[:, :])
        adma(STATES[1][:].rearrange("p a b -> p (a b)"), zerosd[:, :])
        dv(lambda e: nc.vector.memset(HBC3[:].rearrange("p a b -> p (a b)"), 0.0))
        dv(lambda e: nc.vector.tensor_copy(SMALL1[:, 7:8], IDF[:, 0:1]), waits=[(LS, LS.n)])
        dv(lambda e: nc.vector.tensor_copy(SMALL1[:, 7:8], IDF[:, 0:1]), inc=(DV, 1))
        join()
    else:
        if go(): ln_phase(xh)
        if go(): bcdt_phase(halo=True)
        for half in (0, 1):
            if go(): xs_phase(halo=True, half=half)
            if go(): chain_phase(halo=True, half=half)
    if go(): ln_phase(xo)
    dbg_dump("stats", STATS[:])
    dbg_dump("xnfm", xn_fm[:])
    if go(): bcdt_phase(halo=False)
    dbg_dump("dt_raw", dt_raw[:])
    dbg_dump("B_fm", B_fm[:])
    dbg_dump("C_fm", C_fm[:])
    for half in (0, 1):
        if go(): xs_phase(halo=False, half=half)
        if half == 0:
            dbg_dump("xs0", xs[:])
        if go(): chain_phase(halo=False, half=half)
        if half == 0:
            dbg_dump("cbtm", CBTM[:])
            dbg_dump("s16", S16[:])
            dbg_dump("xd32", XD32[:])
            dbg_dump("eax32", EAX32[:])
            dbg_dump("dteb", DTEB[:])
            dbg_dump("acum", ACUM[:])
            dbg_dump("nal", NAL[:])
            dbg_dump("emin", EMIN[:])
    dbg_dump("st0", STATES[0][:])
    dbg_dump("st1", STATES[1][:])
    dbg_dump("Y", Y[:])
    if go(): gate_phase()
    dbg_dump("Yg", Y[:])
    if go(): try_phase()
    if go(): wout_phase()
    dbg_dump("SUM", SUM[:])
    if go(): ffn_phase()
    dbg_dump("SUMF", SUM[:])
    if go(): out_phase()

    with nc.Block() as block:
        P.emit(block)
    ctx.close()
    return nc, P.ninstr



_CACHED = {}


def _consts():
    idf = np.eye(128, dtype=np.float32)
    idb = np.eye(128, dtype=np.float32).astype(BF)
    m1 = np.triu(np.ones((128, 128), dtype=np.float32))
    sel = np.zeros((32, 2048), np.float32)
    for j in range(32):
        sel[j, j * 64:(j + 1) * 64] = 1.0
    sel = np.concatenate([sel, sel], axis=0).astype(BF)
    return idf, idb, m1, sel


def kernel(L0_attn_x, L0_attn_W_in, L0_attn_conv_w, L0_attn_conv_b, L0_attn_dt_bias,
           L0_attn_A_log, L0_attn_D, L0_attn_norm_w, L0_attn_W_out,
           L0_ffn_w1, L0_ffn_w3, L0_ffn_w2, _dbg=(), _cores=8):
    x = np.ascontiguousarray(np.asarray(L0_attn_x, dtype=np.float32))
    w_in = np.ascontiguousarray(np.asarray(L0_attn_W_in, dtype=np.float32))
    w_out = np.ascontiguousarray(np.asarray(L0_attn_W_out, dtype=np.float32))
    w1 = np.ascontiguousarray(np.asarray(L0_ffn_w1, dtype=np.float32))
    w3 = np.ascontiguousarray(np.asarray(L0_ffn_w3, dtype=np.float32))
    w2 = np.ascontiguousarray(np.asarray(L0_ffn_w2, dtype=np.float32))
    conv_w = np.asarray(L0_attn_conv_w, dtype=np.float32)
    conv_b = np.asarray(L0_attn_conv_b, dtype=np.float32)
    dt_bias = np.asarray(L0_attn_dt_bias, dtype=np.float32).reshape(NH, 1)
    a_log = np.asarray(L0_attn_A_log, dtype=np.float32).reshape(NH, 1)
    dvec = np.asarray(L0_attn_D, dtype=np.float32)
    norm_w = np.asarray(L0_attn_norm_w, dtype=np.float32)

    conv_wt = np.ascontiguousarray(conv_w.reshape(NFC, 128, KCV).transpose(1, 0, 2))
    conv_bt = np.ascontiguousarray(conv_b.reshape(NFC, 128).T)
    norm_wt = np.ascontiguousarray(norm_w.reshape(32, 128).T)
    d_rep = np.ascontiguousarray(np.repeat(dvec, HP)[None, :])
    idf, idb, m1, sel = _consts()

    import os as _os
    nohalo = _os.environ.get("NOHALO", "0") == "1"
    stopat = int(_os.environ.get("STOPAT", "99"))
    key = (tuple(sorted(_dbg)), nohalo, stopat)
    if key not in _CACHED:
        _CACHED[key] = build(r_dbg=_dbg, nohalo=nohalo, stopat=stopat)
    nc, ninstr = _CACHED[key]

    in_maps = []
    for c in range(_cores):
        b, r = c // 2, c % 2
        xo = np.ascontiguousarray(x[b, r * T:(r + 1) * T, :])
        xhs = np.ascontiguousarray(x[b, 0:T, :])
        msk = np.full((128, 1), float(r), np.float32)
        in_maps.append(dict(
            xh=xhs, xo=xo, w_in=w_in, w_out=w_out, w1=w1, w3=w3, w2=w2,
            conv_wt=conv_wt, conv_bt=conv_bt, dt_bias=dt_bias, a_log=a_log,
            d_rep=d_rep, norm_wt=norm_wt, maskd=msk, idf=idf, idb=idb,
            m1=m1, sel32=sel, onesd=np.ones((2, 1024), np.float32),
            zerosd=np.zeros((128, 2048), BF)))
    res = run_bass_kernel_spmd(nc, in_maps, core_ids=list(range(_cores)))
    out = np.empty((4, 2048, D), np.float32)
    for c in range(_cores):
        b, r = c // 2, c % 2
        out[b, r * T:(r + 1) * T, :] = res.results[c]["out"]
    if _dbg:
        return out, [{k: v for k, v in rr.items() if k.startswith("dbg_")}
                     for rr in res.results]
    return out
